# version 09
# speedup vs baseline: 1.6543x; 1.0135x over previous
import contextlib
import numpy as np

import concourse.bass as bass
import concourse.mybir as mybir
from concourse.bass_utils import run_bass_kernel_spmd

F32 = mybir.dt.float32
BF16 = mybir.dt.bfloat16
AF = mybir.ActivationFunctionType
ALU = mybir.AluOpType
AX = mybir.AxisListType

ENGINES = ("tensor", "scalar", "vector", "sync", "gpsimd")
DEBUG = False
STAGE = 3

S, D, H, HD = 4096, 1024, 16, 64
TOK = 2048
QS = 1024
BLK = 128
NBLK = QS // BLK
EPS = 1e-5


class Tok:
    __slots__ = ("eng", "idx", "chan", "val", "is_dma")

    def __init__(self, eng, idx, chan=None, val=None, is_dma=False):
        self.eng, self.idx, self.chan, self.val, self.is_dma = eng, idx, chan, val, is_dma


class Bld:

    def __init__(self):
        self.recs = {e: [] for e in ENGINES}
        self.cnt = {}
        self.eng_n = {e: 0 for e in ENGINES}
        self.covered = {e: 0 for e in ENGINES}
        self.last_inc = {e: None for e in ENGINES}
        self.dma_rr = {e: 0 for e in ENGINES}
        self.NPOOL = 16

    def _chan(self, name):
        return self.cnt.setdefault(name, 0)

    def op(self, eng, fn, deps=(), inc=False, dma=False, nodrain=False):
        assert eng in ENGINES
        drain = (not dma) and (not nodrain) and eng in ("vector", "scalar", "gpsimd") \
            and self.eng_n[eng] > self.covered[eng]
        waits = []
        for d in deps:
            if d is None:
                continue
            assert isinstance(d, Tok)
            if d.eng == eng and not d.is_dma:
                if nodrain:
                    assert d.chan is not None, "nodrain same-engine dep needs inc"
                    waits.append((d.chan, d.val))
            else:
                assert d.chan is not None, f"cross-engine dep on {d.eng} op without inc"
                assert d.val <= self.cnt[d.chan], "dep on future value (cycle?)"
                waits.append((d.chan, d.val))
        if drain:
            self.covered[eng] = self.eng_n[eng]
        idx = self.eng_n[eng]
        self.eng_n[eng] += 1
        chan = val = None
        amt = 0
        if dma:
            k = self.dma_rr[eng] % self.NPOOL
            self.dma_rr[eng] += 1
            chan = f"d_{eng}_{k}"
            self._chan(chan)
            if self.cnt[chan] > 0:
                waits.append((chan, self.cnt[chan]))
            self.cnt[chan] += 16
            val, amt = self.cnt[chan], 16
        elif inc:
            chan = "p_" + eng
            self._chan(chan)
            self.cnt[chan] += 1
            val, amt = self.cnt[chan], 1
        self.recs[eng].append((drain, waits, fn, chan, amt))
        tok = Tok(eng, idx, chan, val, dma)
        if chan is not None:
            self.last_inc[eng] = tok
        return tok

    def join(self):
        return [t for t in self.last_inc.values() if t is not None]

    def dma_fence(self, eng):
        out = []
        for k in range(self.NPOOL):
            ch = f"d_{eng}_{k}"
            if self.cnt.get(ch, 0) > 0:
                out.append(Tok(eng, -1, ch, self.cnt[ch], True))
        return out

    def barrier_tok(self, toks):
        return [t for t in toks if t is not None]


class Arena:
    def __init__(self, t, cols):
        self.t = t
        self.free_list = [(0, cols)]
        self.allocs = {}
        self.peak = 0
        self.used = 0

    def alloc(self, name, n, parts=128):
        n = (n + 3) & ~3
        for i, (o, sz) in enumerate(self.free_list):
            if sz >= n:
                if sz == n:
                    self.free_list.pop(i)
                else:
                    self.free_list[i] = (o + n, sz - n)
                self.allocs[name] = (o, n)
                self.used += n
                self.peak = max(self.peak, self.used)
                return self.t[0:parts, o:o + n]
        raise MemoryError(f"arena: cannot alloc {name} ({n} cols); "
                          f"free={self.free_list}")

    def free(self, name):
        o, n = self.allocs.pop(name)
        self.used -= n
        self.free_list.append((o, n))
        self.free_list.sort()
        merged = []
        for seg in self.free_list:
            if merged and merged[-1][0] + merged[-1][1] == seg[0]:
                merged[-1] = (merged[-1][0], merged[-1][1] + seg[1])
            else:
                merged.append(list(seg))
        self.free_list = [tuple(s) for s in merged]


def build_nc():
    nc = bass.Bass("TRN2", target_bir_lowering=False, debug=False)

    xb = nc.dram_tensor("xb", [S, D], F32, kind="ExternalInput").ap()
    wname = {}
    def win(name, shape):
        wname[name] = nc.dram_tensor(name, shape, F32, kind="ExternalInput").ap()
        return wname[name]
    qkv_w = win("qkv_w", [3 * D, D]); qkv_b = win("qkv_b", [3 * D])
    proj_w = win("proj_w", [D, D]); proj_b = win("proj_b", [D])
    iw = win("iw", [D]); mu = win("mu", [D])
    rw1 = win("rw1", [D, 256]); rb1 = win("rb1", [256])
    rw2 = win("rw2", [256, 8]); rb2 = win("rb2", [8])
    hw1 = win("hw1", [D, 256]); hb1 = win("hb1", [256])
    hw2 = win("hw2", [256, 1]); hb2 = win("hb2", [1])
    ew1 = win("ew1", [8, 2 * D, HD]); eb1 = win("eb1", [8, HD])
    ew2 = win("ew2", [8, HD, 3 * D]); eb2 = win("eb2", [8, 3 * D])
    sw1 = win("sw1", [2 * D, HD]); sb1 = win("sb1", [HD])
    sw2 = win("sw2", [HD, 3 * D]); sb2 = win("sb2", [3 * D])
    swt = win("swt", [1, 1])
    fw1 = win("fw1", [D, 2 * D]); fb1 = win("fb1", [2 * D])
    fw2 = win("fw2", [2 * D, D]); fb2 = win("fb2", [D])

    out = nc.dram_tensor("out", [TOK, D], F32, kind="ExternalOutput").ap()

    s_qt = nc.dram_tensor("s_qt", [8, 128, TOK], BF16, kind="Internal").ap()
    s_kt = nc.dram_tensor("s_kt", [8, 128, S], BF16, kind="Internal").ap()
    s_v = nc.dram_tensor("s_v", [16, 128, 32, 65], BF16, kind="Internal").ap()
    s_fw1 = nc.dram_tensor("s_fw1", [16, 128, D], BF16, kind="Internal").ap()
    s_fw2 = nc.dram_tensor("s_fw2", [16, 128, D], BF16, kind="Internal").ap()
    s_ew1 = nc.dram_tensor("s_ew1", [16, 128, 512], BF16, kind="Internal").ap()
    s_ew2 = nc.dram_tensor("s_ew2", [4, 128, 3 * D], BF16, kind="Internal").ap()
    s_x2 = nc.dram_tensor("s_x2", [TOK, D], F32, kind="Internal").ap()
    s_dbg = nc.dram_tensor("s_dbg", [128, 8 * QS], BF16, kind="Internal").ap()

    b = Bld()
    es = contextlib.ExitStack()
    af = Arena(es.enter_context(nc.sbuf_tensor("af", [128, 53184], F32)), 53184)

    class BfArena:
        def alloc(self, name, n, parts=128):
            n = (n + 1) & ~1
            return af.alloc("bf_" + name, n // 2, parts=parts).bitcast(BF16)
        def free(self, name):
            af.free("bf_" + name)
    ab = BfArena()
    pf = es.enter_context(nc.psum_tensor("pf", [128, 3584], F32))
    pb = es.enter_context(nc.psum_tensor("pb", [128, 1024], BF16))
    sems = {}
    for cn in (["p_tensor", "p_scalar", "p_vector", "p_sync", "p_gpsimd"]
               + [f"d_sync_{k}" for k in range(16)]
               + [f"d_scalar_{k}" for k in range(16)]):
        sems[cn] = es.enter_context(nc.semaphore(cn))
    block = es.enter_context(nc.Block())

    def bank(i, n=1):
        return pf[:, i * 512:(i + n) * 512]

    identb = ab.alloc("identb", 128)
    onesb = ab.alloc("onesb", 128)
    onesf = af.alloc("onesf", 128)
    eps_pp = af.alloc("eps_pp", 4)
    rowb = af.alloc("rowb", 128)
    pidx = af.alloc("pidx", 4)

    t_io1 = b.op("gpsimd", lambda g: g.iota(rowb, [[1, 128]], channel_multiplier=0,
                                            allow_small_or_imprecise_dtypes=True), inc=True)
    t_io2 = b.op("gpsimd", lambda g: g.iota(pidx[:, 0:1], [[0, 1]], channel_multiplier=1,
                                            allow_small_or_imprecise_dtypes=True), inc=True)
    t_ms = b.op("vector", lambda v: (v.memset(onesb[0:1, :], 1.0), v.memset(onesf[0:1, :], 1.0),
                                     v.memset(eps_pp[:, 0:1], EPS))[-1], inc=True)
    identf = af.alloc("identf", 128)
    t_idf = b.op("vector", lambda v: v.tensor_tensor(identf, rowb,
                                                     pidx[:, 0:1].broadcast_to([128, 128]),
                                                     ALU.is_equal),
                 deps=[t_io1, t_io2], inc=True)
    t_id = b.op("vector", lambda v: v.tensor_copy(identb, identf), deps=[t_idf], inc=True)

    sselb = ab.alloc("sselb", 512, parts=8)
    t_sio = b.op("gpsimd", lambda g: g.iota(
        sselb[0:8, :].rearrange("p (a c) -> p a c", a=8), [[1, 8], [0, 64]],
        channel_multiplier=0, allow_small_or_imprecise_dtypes=True), inc=True)
    t_sselb = b.op("vector", lambda v: v.tensor_scalar(
        sselb[0:8, :], sselb[0:8, :], pidx[0:8, 0:1], None, ALU.is_equal),
        deps=[t_sio, t_io2], inc=True)
    s2selb = ab.alloc("s2selb", 8)
    def _fill_s2(v):
        v.memset(s2selb[0:128, 0:2], 0.0)
        v.drain()
        v.memset(s2selb[0:64, 0:1], 1.0)
        v.drain()
        return v.memset(s2selb[64:128, 1:2], 1.0)
    t_s2g = b.op("vector", _fill_s2, inc=True)

    def dma(dst, src, deps=(), q="sync"):
        return b.op(q, lambda e, d=dst, s=src: e.dma_start(
            out=d, in_=s, allow_slow_non_contiguous=True), deps=deps, dma=True)

    def cast(dst, src, deps=()):
        return b.op("vector", lambda v, d=dst, s=src: v.tensor_copy(d, s), deps=deps, inc=True)

    stgA = af.alloc("stgA", 3072)
    stgB = af.alloc("stgB", 3072)
    stg_free = {0: None, 1: None}
    stg_map = {0: stgA, 1: stgB}
    stg_i = [0]

    def loadcast(dst_bf, src, rows=128, cols=None, deps=(), slot=None):
        i = (stg_i[0] & 1) if slot is None else slot
        stg_i[0] += 1
        stg = stg_map[i][0:rows, 0:cols]
        dld = dma(stg, src, deps=list(deps) + ([stg_free[i]] if stg_free[i] else []))
        ct = cast(dst_bf, stg, deps=[dld])
        stg_free[i] = ct
        return ct

    qkvb_pp = af.alloc("qkvb_pp", 24)
    qkvb_v = qkv_b.rearrange("(c p) -> p c", p=128)
    t_qkvbpp_l = [dma(qkvb_pp[:, gg * 8:(gg + 1) * 8], qkvb_v[:, gg * 8:(gg + 1) * 8])
                  for gg in range(3)]
    qkvb_vrow = ab.alloc("qkvb_vrow", 1024)
    t_qbv = loadcast(qkvb_vrow[0:1, :], qkv_b[2048:3072].unsqueeze(0), rows=1, cols=1024)
    projb_r = ab.alloc("projb_r", 1024)
    t_pbr = loadcast(projb_r[0:1, :], proj_b.unsqueeze(0), rows=1, cols=1024)
    fb2_r = ab.alloc("fb2_r", 1024)
    t_fb2 = loadcast(fb2_r[0:1, :], fb2.unsqueeze(0), rows=1, cols=1024)
    sb2_r = ab.alloc("sb2_r", 3072)
    t_sb2 = loadcast(sb2_r[0:1, :], sb2.unsqueeze(0), rows=1, cols=3072)
    rb2_r = ab.alloc("rb2_r", 8)
    t_rb2 = loadcast(rb2_r[0:1, :], rb2.unsqueeze(0), rows=1, cols=8)
    hb2_r = ab.alloc("hb2_r", 4)
    t_hb2 = loadcast(hb2_r[0:1, 0:1], hb2.unsqueeze(0), rows=1, cols=1)
    eb2b = ab.alloc("eb2b", 3072, parts=8)
    t_eb2 = loadcast(eb2b[0:8, :], eb2, rows=8, cols=3072)

    def pp_bias(name, src, n, parts=128):
        bp = af.alloc(name, n)
        bs = af.alloc(name + "s", n)
        if n > 8:
            tls = [dma(bp[0:parts, gg:gg + 8], src[:, gg:gg + 8])
                   for gg in range(0, n, 8)]
        else:
            tls = [dma(bp[0:parts, 0:n], src)]
        ts_ = b.op("vector", lambda v, d=bs[0:parts, 0:n], s=bp[0:parts, 0:n]:
                   v.tensor_scalar(d, s, 0.70710678, None, ALU.mult), deps=tls, inc=True)
        return bp, bs, ts_
    rb1_pp, rb1s_pp, t_rb1 = pp_bias("rb1_pp", rb1.rearrange("(c p) -> p c", p=128), 2)
    hb1_pp, hb1s_pp, t_hb1 = pp_bias("hb1_pp", hb1.rearrange("(c p) -> p c", p=128), 2)
    sb1_pp, sb1s_pp, t_sb1 = pp_bias("sb1_pp", sb1.unsqueeze(1), 1, parts=64)
    eb1_pp, eb1s_pp, t_eb1 = pp_bias(
        "eb1_pp", eb1.rearrange("e j -> (e j)").rearrange("(c p) -> p c", p=128), 4)
    fb1_pp, fb1s_pp, t_fb1 = pp_bias("fb1_pp", fb1.rearrange("(c p) -> p c", p=128), 16)


    swrow = af.alloc("swrow", 4)
    t_swl = dma(swrow[0:1, 0:1], swt)
    t_sws = b.op("scalar", lambda sc: sc.activation(out=swrow[0:1, 1:2], in_=swrow[0:1, 0:1],
                                                    func=AF.Sigmoid), deps=[t_swl], inc=True)
    swpp = af.alloc("swpp", 4)
    t_swm = b.op("tensor", lambda t: t.matmul(pf[0:128, 0:1], onesf[0:1, :],
                                              swrow[0:1, 1:2], start=True, stop=True),
                 deps=[t_sws, t_ms], inc=True)
    t_swc = b.op("vector", lambda v: v.tensor_copy(swpp[:, 0:1], pf[0:128, 0:1]),
                 deps=[t_swm], inc=True)
    t_osw = b.op("vector", lambda v: v.tensor_scalar(swpp[:, 1:2], swpp[:, 0:1], -1.0, 1.0,
                                                     ALU.mult, ALU.add), deps=[t_swc], inc=True)
    sw_ap = swpp[:, 0:1]
    osw_ap = swpp[:, 1:2]

    rw1b = ab.alloc("rw1b", 2048)
    t_rw1 = loadcast(rw1b, rw1.rearrange("(c p) n -> p c n", p=128), cols=2048)
    hw1b = ab.alloc("hw1b", 2048)
    t_hw1 = loadcast(hw1b, hw1.rearrange("(c p) n -> p c n", p=128), cols=2048)
    sw1b = ab.alloc("sw1b", 1024)
    t_sw1 = loadcast(sw1b, sw1.rearrange("(c p) n -> p c n", p=128), cols=1024)
    sw2b = ab.alloc("sw2b", 3072, parts=64)
    t_sw2 = loadcast(sw2b[0:64, :], sw2, rows=64, cols=3072)
    rw2b = ab.alloc("rw2b", 16)
    t_rw2 = loadcast(rw2b, rw2.rearrange("(c p) n -> p c n", p=128), cols=16)
    hw2b = ab.alloc("hw2b", 4)
    t_hw2 = loadcast(hw2b[:, 0:2], hw2.rearrange("(c p) n -> p c n", p=128), cols=2)

    pb_last = [None]

    def build_transposed(name, src_w, nrt, ncols_chunks, deps=()):
        dst = ab.alloc(name, 8 * nrt * 128)
        dview = dst.rearrange("p (c q) -> p c q", c=8)
        toks = []
        for rt in range(nrt):
            i = stg_i[0] & 1
            stg_i[0] += 1
            stf = stg_map[i][:, 0:1024]
            dl = dma(stf, src_w[rt * 128:(rt + 1) * 128, :],
                     deps=([stg_free[i]] if stg_free[i] else []))
            stbf = ab_stg[i][:, 0:1024]
            ctk = cast(stbf, stf, deps=[dl])
            stg_free[i] = ctk
            ttr = b.op("tensor", lambda t, sb=stbf: [
                t.transpose(pb[:, c * 128:(c + 1) * 128], sb[:, c * 128:(c + 1) * 128], identb)
                for c in range(8)][-1],
                deps=[ctk, t_id] + ([pb_last[0]] if pb_last[0] else []), inc=True)
            tev = b.op("vector", lambda v, rt=rt: v.tensor_copy(
                dview[:, :, rt * 128:(rt + 1) * 128], pb[:, 0:1024]), deps=[ttr], inc=True)
            pb_last[0] = tev
            toks.append(tev)
        return dst, toks

    ab_stg = [ab.alloc("ab_stg0", 3072), ab.alloc("ab_stg1", 3072)]
    qkvwT, t_qkvwT = build_transposed("qkvwT", qkv_w, 24, 8)

    def cast_spill(dst_dram, src, n, cols, rearr=None):
        toks = []
        for rt in range(n):
            i = stg_i[0] & 1
            stg_i[0] += 1
            stf = stg_map[i][:, 0:cols]
            src_sl = src(rt)
            dl = dma(stf, src_sl, deps=([stg_free[i]] if stg_free[i] else []))
            stbf = ab_stg[i][:, 0:cols]
            ctk = cast(stbf, stf, deps=[dl])
            sp = dma(dst_dram[rt], stbf, deps=[ctk])
            stg_free[i] = sp
            toks.append(sp)
        return toks
    t_sfw1 = cast_spill(s_fw1, lambda t: fw1[(t // 2) * 128:(t // 2 + 1) * 128, (t % 2) * 1024:(t % 2 + 1) * 1024], 16, 1024)
    t_sfw2 = cast_spill(s_fw2, lambda rt: fw2[rt * 128:(rt + 1) * 128, :], 16, 1024)
    t_sew1 = cast_spill(
        s_ew1, lambda kc: ew1[:, kc * 128:(kc + 1) * 128, :].rearrange("e p j -> p e j"),
        16, 512)

    af.free("stgA")
    af.free("stgB")

    ab.free("ab_stg0")
    ab.free("ab_stg1")
    XnT = ab.alloc("XnT", 8 * S)
    XnTv = XnT.rearrange("p (c n) -> p c n", c=8)
    qkvwTv = qkvwT.rearrange("p (c n) -> p c n", c=8)

    xt_s = [af.alloc("xt0", 1024), af.alloc("xt1", 1024)]
    xn_s = [ab.alloc("xn0", 1024), ab.alloc("xn1", 1024)]
    sqb = ab.alloc("sqb", S)
    sqgarb = sqb[:, 0:1024]
    st_s = af.alloc("st_small", 16)
    xt_free = [None, None]
    xn_free = [None, None]
    prev_ssq = None
    prev_stat = None

    for rt in range(32):
        i = rt & 1
        tx = dma(xt_s[i], xb[rt * 128:(rt + 1) * 128, :],
                 deps=([xt_free[i]] if xt_free[i] else []))
        tsum = b.op("vector", lambda v, x=xt_s[i]: v.reduce_sum(st_s[:, 0:1], x, axis=AX.X),
                    deps=[tx, prev_stat], inc=True)
        tssq = b.op("scalar", lambda sc, x=xt_s[i]: sc.activation(
            out=sqgarb, in_=x, func=AF.Square, accum_out=st_s[:, 1:2]),
            deps=[tx, prev_ssq, prev_stat], inc=True)
        prev_ssq = tssq
        tmean = b.op("vector", lambda v: v.tensor_scalar(st_s[:, 2:3], st_s[:, 0:1],
                                                         1.0 / D, None, ALU.mult),
                     deps=[tsum], inc=True)
        tmsq = b.op("vector", lambda v: v.tensor_tensor(st_s[:, 3:4], st_s[:, 2:3],
                                                        st_s[:, 2:3], ALU.mult),
                    deps=[tmean], inc=True)
        tvar = b.op("vector", lambda v: v.scalar_tensor_tensor(
            st_s[:, 4:5], st_s[:, 1:2], 1.0 / D, st_s[:, 3:4], ALU.mult, ALU.subtract),
            deps=[tmsq, tssq], inc=True)
        tln = b.op("scalar", lambda sc: sc.activation(out=st_s[:, 5:6], in_=st_s[:, 4:5],
                                                      func=AF.Ln, bias=eps_pp[:, 0:1]),
                   deps=[tvar, t_ms], inc=True)
        trs = b.op("scalar", lambda sc: sc.activation(out=st_s[:, 5:6], in_=st_s[:, 5:6],
                                                      func=AF.Exp, scale=-0.5),
                   deps=[tln], inc=True)
        txn = b.op("vector", lambda v, x=xt_s[i], d=xn_s[i]: v.tensor_scalar(
            d, x, st_s[:, 2:3], st_s[:, 5:6], ALU.subtract, ALU.mult),
            deps=[trs, tmean, xn_free[i]], inc=True)
        prev_stat = txn
        xt_free[i] = txn
        ttr = b.op("tensor", lambda t, sb=xn_s[i]: [
            t.transpose(pb[:, c * 128:(c + 1) * 128], sb[:, c * 128:(c + 1) * 128], identb)
            for c in range(8)][-1],
            deps=[txn, t_id] + ([pb_last[0]] if pb_last[0] else []), inc=True)
        txev = b.op("vector", lambda v, rt=rt: v.tensor_copy(
            XnTv[:, :, rt * 128:(rt + 1) * 128], pb[:, 0:1024]), deps=[ttr], inc=True)
        pb_last[0] = txev
        xn_free[i] = txev

    t_xnT_done = txev
    for nm in ("xt0", "xt1", "bf_xn0", "bf_xn1", "st_small"):
        af.free(nm)

    QTas = ab.alloc("QTas", TOK)
    KTas = ab.alloc("KTas", S)
    qn2sb = af.alloc("qn2sb", TOK, parts=16)
    lse_sb = af.alloc("lse_sb", TOK, parts=16)
    knmax = af.alloc("knmax", 8)
    C8 = ab.alloc("C8", TOK, parts=16)
    qrow2 = af.alloc("qrow2", 1024, parts=2)
    kmx2 = af.alloc("kmx2", 8, parts=2)
    c0s = af.alloc("c0s", 4, parts=3)
    c0row12 = af.alloc("c0row12", 4, parts=1)
    c0col = af.alloc("c0col", 4, parts=16)
    c0bc = af.alloc("c0bc", 4)
    c0bc_free = [None]
    c0_dma_prev = []
    qm2run = af.alloc("qm2run", 4, parts=2)
    Esb = ab.alloc("Esb", 512)
    colsel = ab.alloc("colsel", 256)
    def _fill_colsel(v):
        v.memset(colsel[0:128, :], 0.0)
        v.drain()
        for vv in range(15):
            v.memset(colsel[0:128, vv * 16 + vv:vv * 16 + vv + 1], 1.0)
        return v.memset(colsel[0:128, 15 * 16 + 15:15 * 16 + 16], 1.0)
    t_colsel = b.op("vector", _fill_colsel, inc=True)
    t_lse0 = b.op("vector", lambda v: v.memset(lse_sb[0:16, :], 0.0), inc=True)

    def mm_chain(t, psum_ap, lhs_list, rhs_list, start=True, stop_last=True):
        n = len(lhs_list)
        last = None
        for k in range(n):
            last = t.matmul(psum_ap, lhs_list[k], rhs_list[k],
                            start=(start and k == 0),
                            stop=(stop_last and k == n - 1))
        return last

    q_bank_free = [None, None]
    k_bank_free = [None, None]
    sel_free = [None]
    sqb_readers = []
    qtas_readers = []
    ktas_readers = []
    qspill = []
    kspill = []
    smax_dmas = []
    qn2_dmas = []
    kmx_dmas = []
    for j in range(8):
        qtas_war = list(qtas_readers)
        ev_nt = []
        for nt in range(4):
            bk = nt & 1
            mmq = b.op("tensor", lambda t, j=j, nt=nt, bk=bk: mm_chain(
                t, pf[:, bk * 512:(bk + 1) * 512],
                [qkvwTv[:, kc, j * 128:(j + 1) * 128] for kc in range(8)],
                [XnTv[:, kc, nt * 512:(nt + 1) * 512] for kc in range(8)]),
                deps=[t_xnT_done, t_qkvwT[-1], q_bank_free[bk]], inc=True)
            evq = b.op("vector", lambda v, nt=nt, bk=bk, j=j: v.tensor_scalar(
                QTas[:, nt * 512:(nt + 1) * 512], pf[:, bk * 512:(bk + 1) * 512],
                qkvb_pp[:, j:j + 1], None, ALU.add),
                deps=[mmq] + t_qkvbpp_l + qtas_war, inc=True)
            q_bank_free[bk] = evq
            ev_nt.append(evq)
        tqsq = b.op("scalar", lambda sc: sc.activation(out=sqb[:, 0:TOK], in_=QTas,
                                                       func=AF.Square),
                    deps=ev_nt + sqb_readers, inc=True)
        for half in range(2):
            tselq = b.op("tensor", lambda t, half=half: [t.matmul(
                pf[0:2, 2048 + nt * 512:2048 + (nt + 1) * 512], s2selb[:, 0:2],
                sqb[:, (2 * half + nt) * 512:(2 * half + nt + 1) * 512],
                start=True, stop=True) for nt in range(2)][-1],
                deps=[tqsq, t_s2g, sel_free[0]], inc=True)
            tevq2 = b.op("vector", lambda v: v.tensor_copy(
                qrow2[0:2, 0:1024], pf[0:2, 2048:3072]),
                deps=[tselq] + qn2_dmas[-1:], inc=True)
            sel_free[0] = tevq2
            tqmx = b.op("vector", lambda v, half=half: v.reduce_max(
                qm2run[0:2, half:half + 1], qrow2[0:2, 0:1024], axis=AX.X),
                deps=[tevq2], inc=True)
            qn2_dmas.append(dma(qn2sb[2 * j:2 * j + 2, half * 1024:(half + 1) * 1024],
                                qrow2[0:2, 0:1024], deps=[tevq2]))
        tspq = dma(s_qt[j], QTas, deps=ev_nt)
        qspill.append(tspq)

        ktas_war = list(ktas_readers)
        ev_ntk = []
        for nt in range(8):
            bk = 2 + (nt & 1)
            mmk = b.op("tensor", lambda t, j=j, nt=nt, bk=bk: mm_chain(
                t, pf[:, bk * 512:(bk + 1) * 512],
                [qkvwTv[:, kc, 1024 + j * 128:1024 + (j + 1) * 128] for kc in range(8)],
                [XnTv[:, kc, nt * 512:(nt + 1) * 512] for kc in range(8)]),
                deps=[t_xnT_done, k_bank_free[nt & 1]], inc=True)
            evk = b.op("vector", lambda v, nt=nt, bk=bk, j=j: v.tensor_scalar(
                KTas[:, nt * 512:(nt + 1) * 512], pf[:, bk * 512:(bk + 1) * 512],
                qkvb_pp[:, 8 + j:9 + j], None, ALU.add),
                deps=[mmk] + t_qkvbpp_l + ktas_war, inc=True)
            k_bank_free[nt & 1] = evk
            ev_ntk.append(evk)
        tksq = b.op("scalar", lambda sc: sc.activation(out=sqb[:, 0:S], in_=KTas,
                                                       func=AF.Square),
                    deps=ev_ntk + [tselq] + sqb_readers, inc=True)
        k_sel_toks = []
        tkm = None
        for r in range(4):
            tselk = b.op("tensor", lambda t, r=r: [t.matmul(
                pf[0:2, 2048 + nt * 512:2048 + (nt + 1) * 512], s2selb[:, 0:2],
                sqb[:, (2 * r + nt) * 512:(2 * r + nt + 1) * 512],
                start=True, stop=True) for nt in range(2)][-1],
                deps=[tksq, t_s2g, sel_free[0]], inc=True)
            tkm = b.op("vector", lambda v, r=r: v.reduce_max(
                kmx2[0:2, r:r + 1], pf[0:2, 2048:3072], axis=AX.X),
                deps=[tselk] + (kmx_dmas[-1:] if r == 0 else []), inc=True)
            sel_free[0] = tkm
            k_sel_toks.append(tselk)
        kmx_dmas.append(dma(knmax[2 * j:2 * j + 2, 0:4], kmx2[0:2, 0:4], deps=[tkm]))
        tspk = dma(s_kt[j], KTas, deps=ev_ntk)
        kspill.append(tspk)
        sqb_readers = k_sel_toks[-2:]

        tqm = b.op("vector", lambda v: v.reduce_max(c0s[0:2, 0:1], qm2run[0:2, 0:2],
                                                    axis=AX.X),
                   deps=[tqmx] + c0_dma_prev, inc=True)
        tkm0 = b.op("vector", lambda v: v.reduce_max(c0s[0:2, 1:2], kmx2[0:2, 0:4],
                                                     axis=AX.X), deps=[tkm], inc=True)
        tprod = b.op("vector", lambda v: v.tensor_tensor(c0s[0:2, 2:3], c0s[0:2, 0:1],
                                                         c0s[0:2, 1:2], ALU.mult),
                     deps=[tqm, tkm0], inc=True)
        tsq1 = b.op("scalar", lambda sc: sc.activation(out=c0s[0:2, 2:3],
                                                       in_=c0s[0:2, 2:3], func=AF.Ln,
                                                       bias=eps_pp[0:2, 0:1]),
                    deps=[tprod], inc=True)
        tsq2 = b.op("scalar", lambda sc: sc.activation(out=c0s[0:2, 2:3],
                                                       in_=c0s[0:2, 2:3], func=AF.Exp,
                                                       scale=0.5), deps=[tsq1], inc=True)
        tc0 = b.op("vector", lambda v: v.tensor_scalar(c0s[0:2, 3:4], c0s[0:2, 2:3],
                                                       0.125, None, ALU.mult),
                   deps=[tsq2], inc=True)
        tnc0 = b.op("vector", lambda v: v.tensor_scalar(c0s[0:2, 0:1], c0s[0:2, 3:4],
                                                        -1.0, None, ALU.mult),
                    deps=[tc0], inc=True)
        d_c0c = dma(c0col[2 * j:2 * j + 2, 0:1], c0s[0:2, 3:4], deps=[tc0, tnc0])
        d_c0r = dma(c0row12[0:1, 0:2], c0s[0:2, 0:1], deps=[tnc0])
        c0_dma_prev[:] = [d_c0c, d_c0r]
        mmc0 = b.op("tensor", lambda t: t.matmul(pf[0:128, 3456:3458], onesf[0:1, :],
                                                 c0row12[0:1, 0:2], start=True, stop=True),
                    deps=[d_c0r, t_ms, c0bc_free[0]], inc=True)
        tc0bc = b.op("vector", lambda v: v.tensor_copy(c0bc[:, 0:2], pf[0:128, 3456:3458]),
                     deps=[mmc0], inc=True)
        kts = KTas.rearrange("p (a s) -> p s a", s=32)
        last_sum = None
        for hh in range(2):
            r0, r1 = hh * 64, hh * 64 + 64
            for nt in range(4):
                mms = b.op("tensor", lambda t, hh=hh, nt=nt: t.matmul(
                    pf[:, 3072:3584], kts[hh * 64:hh * 64 + 64, 0, :],
                    QTas[hh * 64:hh * 64 + 64, nt * 512:(nt + 1) * 512],
                    start=True, stop=True,
                    tile_position=(64, 0) if hh else (0, 0)),
                    deps=[ev_nt[-1], ev_ntk[-1], last_sum, tc0bc], inc=True)
                texp = b.op("scalar", lambda sc: sc.activation(
                    out=Esb[:, 0:512], in_=pf[:, 3072:3584], func=AF.Exp, scale=0.125,
                    bias=c0bc[:, hh:hh + 1]),
                    deps=[mms, tc0bc], inc=True)
                mmsum = b.op("tensor", lambda t, hh=hh, j=j: t.matmul(
                    pf[0:16, 3072:3584], colsel[:, (2 * j + hh) * 16:(2 * j + hh + 1) * 16],
                    Esb[:, 0:512], start=True, stop=True), deps=[texp], inc=True)
                tadd = b.op("vector", lambda v, nt=nt: v.tensor_tensor(
                    lse_sb[0:16, nt * 512:(nt + 1) * 512],
                    lse_sb[0:16, nt * 512:(nt + 1) * 512],
                    pf[0:16, 3072:3584], ALU.add), deps=[mmsum, t_lse0], inc=True)
                last_sum = tadd
        smax_dmas.append(last_sum)
        c0bc_free[0] = last_sum
        qtas_readers = [tspq, mms, tqsq]
        ktas_readers = [tspk, mms, tksq]

    Vsb = ab.alloc("Vsb", 16 * 65 + 14)
    Vsbv = Vsb[:, 0:1040].rearrange("p (h c) -> p h c", h=16)
    t_vones = b.op("vector", lambda v: v.memset(Vsb[:, 0:1040], 1.0), inc=True)
    v_bank_free = [q_bank_free[0]]
    v_bank_free2 = [q_bank_free[1]]
    v_evac_prev = [t_vones]
    vspill_last = None
    for rt in range(32):
        mmv = b.op("tensor", lambda t, rt=rt: (
            mm_chain(t, pf[:, 0:512],
                     [XnTv[:, kc, rt * 128:(rt + 1) * 128] for kc in range(8)],
                     [qkvwTv[:, kc, 2048:2560] for kc in range(8)], stop_last=False),
            mm_chain(t, pf[:, 512:1024],
                     [XnTv[:, kc, rt * 128:(rt + 1) * 128] for kc in range(8)],
                     [qkvwTv[:, kc, 2560:3072] for kc in range(8)], stop_last=False),
            t.matmul(pf[:, 0:512], onesb[0:1, :], qkvb_vrow[0:1, 0:512],
                     start=False, stop=True),
            t.matmul(pf[:, 512:1024], onesb[0:1, :], qkvb_vrow[0:1, 512:1024],
                     start=False, stop=True))[-1],
            deps=[t_xnT_done, t_qbv, t_ms, v_bank_free[0], v_bank_free2[0]], inc=True)
        evv = b.op("vector", lambda v, rt=rt: v.tensor_copy(
            Vsbv[:, :, 0:64], pf[:, 0:1024].rearrange("p (h c) -> p h c", c=64)),
            deps=[mmv] + v_evac_prev, inc=True)
        v_bank_free[0] = evv
        v_bank_free2[0] = evv
        spvs = []
        for g in range(4):
            spvs.append(dma(s_v[g * 4:(g + 1) * 4, :, rt, :].rearrange("h p c -> p h c"),
                            Vsbv[:, g * 4:(g + 1) * 4, :], deps=[evv]))
        v_evac_prev = list(spvs)
        vspill_last = spvs[-1]

    qnl = qn2sb
    mk8 = af.alloc("mk8", 4)
    t_c1 = b.op("scalar", lambda sc: sc.activation(out=qnl[0:16, :], in_=qn2sb[0:16, :],
                                                   func=AF.Ln, bias=eps_pp[0:16, 0:1]),
                deps=[qn2_dmas[-1], t_ms], inc=True)
    t_c2 = b.op("scalar", lambda sc: sc.activation(out=qnl[0:16, :], in_=qnl[0:16, :],
                                                   func=AF.Exp, scale=0.5),
                deps=[t_c1], inc=True)
    t_c3 = b.op("vector", lambda v: v.reduce_max(knmax[0:16, 4:5], knmax[0:16, 0:4],
                                                 axis=AX.X), deps=[kmx_dmas[-1]], inc=True)
    t_c4 = b.op("scalar", lambda sc: sc.activation(out=mk8[0:16, 0:1], in_=knmax[0:16, 4:5],
                                                   func=AF.Ln, bias=eps_pp[0:16, 0:1]),
                deps=[t_c3], inc=True)
    t_c5 = b.op("scalar", lambda sc: sc.activation(out=mk8[0:16, 0:1], in_=mk8[0:16, 0:1],
                                                   func=AF.Exp, scale=0.5),
                deps=[t_c4], inc=True)
    t_c6 = b.op("vector", lambda v: v.tensor_scalar(mk8[0:16, 1:2], mk8[0:16, 0:1],
                                                    0.125, None, ALU.mult),
                deps=[t_c5], inc=True)
    t_c7 = b.op("vector", lambda v: v.tensor_scalar(qnl[0:16, :], qnl[0:16, :],
                                                    mk8[0:16, 1:2], None, ALU.mult),
                deps=[t_c6, t_c2], inc=True)
    t_b38 = b.op("vector", lambda v: v.memset(mk8[0:16, 2:3], 1.3e-38), inc=True)
    t_lseln = b.op("scalar", lambda sc: sc.activation(out=lse_sb[0:16, :],
                                                      in_=lse_sb[0:16, :], func=AF.Ln,
                                                      bias=mk8[0:16, 2:3]),
                   deps=smax_dmas[-1:] + [t_c7, t_b38], inc=True)
    t_c8a = b.op("vector", lambda v: v.tensor_scalar(lse_sb[0:16, :], lse_sb[0:16, :],
                                                     c0col[0:16, 0:1], None, ALU.add),
                 deps=[t_lseln, d_c0c], inc=True)
    t_c8b = b.op("vector", lambda v: v.tensor_tensor(lse_sb[0:16, :], lse_sb[0:16, :],
                                                     qnl[0:16, :], ALU.min),
                 deps=[t_c8a], inc=True)
    t_c8c = b.op("vector", lambda v: v.tensor_scalar(lse_sb[0:16, :], lse_sb[0:16, :],
                                                     1.0, 20.0, ALU.mult, ALU.add),
                 deps=[t_c8b], inc=True)
    t_c8d = b.op("vector", lambda v: v.tensor_scalar(qnl[0:16, :], qnl[0:16, :],
                                                     1.0, -60.0, ALU.mult, ALU.add),
                 deps=[t_c8b], inc=True)
    t_c8e = b.op("vector", lambda v: v.tensor_tensor(qnl[0:16, :], qnl[0:16, :],
                                                     lse_sb[0:16, :], ALU.max),
                 deps=[t_c8c, t_c8d], inc=True)
    t_c8 = b.op("vector", lambda v: v.tensor_scalar(C8[0:16, :], qnl[0:16, :],
                                                    -8.0, None, ALU.mult),
                deps=[t_c8e], inc=True)

    for nm in ("bf_XnT", "bf_qkvwT", "bf_QTas", "bf_KTas", "bf_sqb", "bf_Vsb",
               "qn2sb", "lse_sb", "knmax", "mk8", "qkvb_pp", "c0s", "c0row12",
               "c0col", "c0bc", "qm2run", "bf_Esb", "bf_colsel",
               "identf", "rowb", "pidx", "qrow2", "kmx2"):
        af.free(nm)

    def full_fence():
        return b.op("sync", lambda e: e.nop(),
                    deps=b.join() + b.dma_fence("sync"), inc=True)

    t_fence1 = full_fence()

    stg_map[0] = af.alloc("stgA", 3072)
    stg_map[1] = af.alloc("stgB", 2048)
    ab_stg[0] = ab.alloc("ab_stg0", 3072)
    ab_stg[1] = ab.alloc("ab_stg1", 3072)
    murow = af.t[64:65, af.allocs["bf_sb2_r"][0]:af.allocs["bf_sb2_r"][0] + 1024]
    iwrow = af.t[64:65, af.allocs["bf_eb2b"][0]:af.allocs["bf_eb2b"][0] + 1024]
    fb1_r = af.t[64:65, af.allocs["bf_sw2b"][0]:af.allocs["bf_sw2b"][0] + 1024] \
        .bitcast(BF16)
    onesf64 = af.t[64:65, af.allocs["onesf"][0]:af.allocs["onesf"][0] + 128]
    onesb64 = af.t[64:65, af.allocs["bf_onesb"][0]:af.allocs["bf_onesb"][0] + 64] \
        .bitcast(BF16)
    t_ms64 = b.op("vector", lambda v: (v.memset(onesf64[0:1, :], 1.0),
                                       v.memset(onesb64[0:1, :], 1.0))[-1],
                  deps=[t_sb2, t_eb2, t_sw1], inc=True)
    t_mur = dma(murow[0:1, :], mu.unsqueeze(0), deps=[t_sb2])
    t_iwr = dma(iwrow[0:1, :], iw.unsqueeze(0), deps=[t_eb2])
    mubc = af.alloc("mubc", 1024)
    iwbc = af.alloc("iwbc", 1024)

    def bcast_row(dst, row, deps):
        mma = b.op("tensor", lambda t, d0=row: t.matmul(
            bank(0), onesf64[0:1, :], d0[0:1, 0:512], start=True, stop=True),
            deps=deps + [t_ms64, q_bank_free[0], q_bank_free[1], v_bank_free[0]],
            inc=True)
        mmb = b.op("tensor", lambda t, d0=row: t.matmul(
            bank(1), onesf64[0:1, :], d0[0:1, 512:1024], start=True, stop=True),
            deps=[mma], inc=True)
        eva = b.op("vector", lambda v, d=dst: v.tensor_copy(d[:, 0:512], bank(0)),
                   deps=[mma], inc=True)
        ev = b.op("vector", lambda v, d=dst: v.tensor_copy(d[:, 512:1024], bank(1)),
                  deps=[mmb, eva], inc=True)
        return ev
    t_mubc = bcast_row(mubc, murow, [t_mur])
    t_iwbc = bcast_row(iwbc, iwrow, [t_iwr, t_mubc])
    projwT, t_projwT = build_transposed("projwT", proj_w, 8, 8)
    _stg64 = stg_map[0][64:65, 0:2048]
    _dld64 = dma(_stg64, fb1.unsqueeze(0),
                 deps=([stg_free[0]] if stg_free[0] else []) + [t_sw1])
    t_fb1r = cast(fb1_r[0:1, :], _stg64, deps=[_dld64])
    stg_free[0] = t_fb1r
    ew2R = ab.alloc("ew2R", 4 * 3072)
    ew2Rv = ew2R.rearrange("p (c n) -> p c n", c=4)
    ew2flat0 = ew2.rearrange("e h n -> (e h) n")
    t_ew2R = [loadcast(ew2Rv[:, rt, :], ew2flat0[rt * 128:(rt + 1) * 128, :], cols=3072,
                       slot=0) for rt in range(4)]
    for nm in ("stgA", "stgB", "bf_ab_stg0", "bf_ab_stg1"):
        af.free(nm)

    slots = {nm: af.alloc("sl_" + nm, 1024) for nm in
             ("intg", "vf", "tf", "vn0", "vn1", "vsh", "acc1", "sC")}
    par = [{nm: af.alloc("sl%d_%s" % (p, nm), 1024) for nm in ("xf", "sA", "sB")}
           for p in range(2)]
    stat2 = [af.alloc("stat%d" % p, 16) for p in range(2)]
    stat = stat2[0]
    fw1t_r = ab.alloc("fw1t", 2048)
    fw1t = [fw1t_r[:, 0:1024], fw1t_r[:, 1024:2048]]
    fw2t_r = ab.alloc("fw2t", 2048)
    fw2t = [fw2t_r[:, 0:1024], fw2t_r[:, 1024:2048]]
    KTa = [ab.alloc("KTa0", S), ab.alloc("KTa1", S)]
    QTa = [ab.alloc("QTa0", QS), ab.alloc("QTa1", QS)]
    Vb = [ab.alloc("Vb0", 32 * 65 + 14), ab.alloc("Vb1", 32 * 65 + 14)]
    Abuf = [ab.alloc("Ab%d" % i, QS) for i in range(2)]
    OT = ab.alloc("OT", 8 * QS)
    Osb = af.alloc("Osb", QS)
    Ostg = ab.alloc("Ostg", QS)
    rcp = par[0]["sA"][0:1, :]
    dnm = par[0]["sB"][0:1, :]
    t_fence2 = full_fence()

    kta_free = [None, None]
    vb_free = [None, None]
    ab_free = [None, None]
    ep_free = [None]
    ostg_free = [None]
    sb_free = [None, None]
    ep6_free = [None]
    ep_st = [None]
    ot_ready = []
    all_stores = []

    def attention_qs(qs, p3_done_deps):
        ot_toks = []
        t_kon = b.op("vector", lambda v: (v.memset(KTa[0][64:65, :], 1.0),
                                          v.memset(KTa[1][64:65, :], 1.0))[-1],
                     deps=[t_fence2] + list(p3_done_deps) + [kta_free[0], kta_free[1]],
                     inc=True)
        pend = [None]
        for h in range(16):
            bi = h & 1
            rows = (h % 2) * 64
            j = h // 2
            tK = dma(KTa[bi][0:64, :], s_kt[j, rows:rows + 64, :],
                     deps=[kspill[j], kta_free[bi], t_kon] + list(p3_done_deps))
            tQ = dma(QTa[bi][0:64, :], s_qt[j, rows:rows + 64, qs * QS:(qs + 1) * QS],
                     deps=[qspill[j], kta_free[bi]] + list(p3_done_deps))
            tC = dma(QTa[bi][64:65, :], C8[h:h + 1, qs * QS:(qs + 1) * QS],
                     deps=[t_c8, kta_free[bi]] + list(p3_done_deps))
            tV = dma(Vb[bi][:, 0:2080], s_v[h],
                     deps=[vspill_last, vb_free[bi]] + list(p3_done_deps))
            Vbv = Vb[bi][:, 0:2080].rearrange("p (a c) -> p a c", c=65)
            mmq = [None] * 32
            tex = [None] * 32
            avb = [None]

            def emit_qk(kt, bi=bi, tK=tK, tQ=tQ, tC=tC):
                sb = kt & 1
                mmq[kt] = b.op("tensor", lambda t, kt=kt, sb=sb, bi=bi: (
                    t.matmul(pf[:, sb * 1024:sb * 1024 + 512],
                             KTa[bi][0:65, kt * 128:(kt + 1) * 128],
                             QTa[bi][0:65, 0:512], start=True, stop=True),
                    t.matmul(pf[:, sb * 1024 + 512:sb * 1024 + 1024],
                             KTa[bi][0:65, kt * 128:(kt + 1) * 128],
                             QTa[bi][0:65, 512:1024], start=True, stop=True))[-1],
                    deps=[tK, tQ, tC, sb_free[sb]] + list(p3_done_deps), inc=True)

            def emit_exp(kt):
                sb = kt & 1
                tex[kt] = b.op("scalar", lambda sc, kt=kt, sb=sb: sc.activation(
                    out=Abuf[kt & 1], in_=pf[:, sb * 1024:(sb + 1) * 1024],
                    func=AF.Exp, scale=0.125),
                    deps=[mmq[kt], ab_free[kt & 1]], inc=True, nodrain=True)
                sb_free[sb] = tex[kt]

            def emit_av(kt, Vbv=Vbv, tV=tV):
                tav = b.op("tensor", lambda t, kt=kt, Vbv=Vbv: (
                    t.matmul(pf[0:65, 2048:2560], Vbv[:, kt, :], Abuf[kt & 1][:, 0:512],
                             start=(kt == 0), stop=(kt == 31)),
                    t.matmul(pf[0:65, 2560:3072], Vbv[:, kt, :], Abuf[kt & 1][:, 512:1024],
                             start=(kt == 0), stop=(kt == 31)))[-1],
                    deps=[tex[kt], tV] + ([ep_free[0]] if kt == 0 else []), inc=True)
                ab_free[kt & 1] = tav
                avb[0] = tav

            emit_qk(0)
            emit_exp(0)
            for kt in range(1, 32):
                emit_qk(kt)
                emit_exp(kt)
                emit_av(kt - 1)
                if kt == 16 and pend[0] is not None:
                    pend[0][0]()
                if kt == 24 and pend[0] is not None:
                    pend[0][1]()
                    pend[0] = None
            emit_av(31)
            av_last = avb[0]
            kta_free[bi] = mmq[31]
            vb_free[bi] = av_last
            t_cpd = b.op("vector", lambda v: v.tensor_copy(dnm[0:1, :],
                                                           pf[64:65, 2048:3072]),
                         deps=[av_last], inc=True)
            t_on = b.op("vector", lambda v: v.tensor_copy(Osb[0:64, :],
                                                          pf[0:64, 2048:3072]),
                        deps=[av_last], inc=True)
            t_rc = b.op("vector", lambda v: v.reciprocal(rcp[0:1, :], dnm[0:1, :]),
                        deps=[t_cpd, av_last], inc=True)
            ep_free[0] = t_on

            def ep_a(h=h, t_rc=t_rc, t_on=t_on):
                base = (h // 2) * 1024
                tb = b.op("tensor", lambda t: t.matmul(
                    pf[0:64, 3072:3584], onesf[0:1, 0:64], rcp[0:1, 0:512],
                    start=True, stop=True), deps=[t_rc, ep6_free[0]], inc=True)
                dst = OT[0:64, base:base + 512] if h % 2 == 0 else Ostg[0:64, 0:512]
                na = b.op("vector", lambda v: v.tensor_tensor(
                    dst, Osb[0:64, 0:512], pf[0:64, 3072:3584], ALU.mult),
                    deps=[tb, t_on] + ([ostg_free[0]] if h % 2 else [])
                    + list(p3_done_deps), inc=True)
                ep_st[0] = na

            def ep_b(h=h, t_rc=t_rc, t_on=t_on):
                base = (h // 2) * 1024
                na = ep_st[0]
                tb2 = b.op("tensor", lambda t: t.matmul(
                    pf[0:64, 3072:3584], onesf[0:1, 0:64], rcp[0:1, 512:1024],
                    start=True, stop=True), deps=[t_rc, na], inc=True)
                dst2 = (OT[0:64, base + 512:base + 1024] if h % 2 == 0
                        else Ostg[0:64, 512:1024])
                nb = b.op("vector", lambda v: v.tensor_tensor(
                    dst2, Osb[0:64, 512:1024], pf[0:64, 3072:3584], ALU.mult),
                    deps=[tb2, t_on] + list(p3_done_deps), inc=True)
                ep6_free[0] = nb
                if h % 2 == 0:
                    ot_toks.append(nb)
                else:
                    t_mv = dma(OT[64:128, base:base + 1024], Ostg[0:64, :], deps=[nb])
                    ostg_free[0] = t_mv
                    ot_toks.append(t_mv)

            pend[0] = (ep_a, ep_b)
        pend[0][0]()
        pend[0][1]()
        pend[0] = None
        return ot_toks

    xfb = Abuf[0]
    xfT = Abuf[1]
    xfTv = xfT.rearrange("p (c n) -> p c n", c=8)
    intb = Vb[0][:, 0:1024]
    intTb = Vb[0][:, 1024:2048]
    intTv = intTb.rearrange("p (c n) -> p c n", c=8)
    vbb = KTa[0][:, 0:1024]
    vTb = KTa[0][:, 1024:2048]
    vTv = vTb.rearrange("p (c n) -> p c n", c=8)
    hrb = KTa[0][:, 2048:4096]
    hrT = KTa[1][:, 0:2048]
    hrTv = hrT.rearrange("p (c n) -> p c n", c=16)
    h1T = KTa[1][:, 2048:2304]
    h1Tv = h1T.rearrange("p (c n) -> p c n", c=2)
    hhT = KTa[1][:, 2304:2560]
    hhTv = hhT.rearrange("p (c n) -> p c n", c=2)
    hallT = KTa[1][:, 2560:3072]
    hallTv = hallT.rearrange("p (c n) -> p c n", c=4)
    hm0T = KTa[1][:, 3072:3584]
    hm0Tv = hm0T.rearrange("p (c n) -> p c n", c=4)
    hm1T = KTa[1][:, 3584:4096]
    hm1Tv = hm1T.rearrange("p (c n) -> p c n", c=4)
    hsT = QTa[0][0:64, 0:128]
    EQ = [dict(b=QTa[0][:, o:o + 16], T0=QTa[0][0:8, o + 16:o + 144],
               T1=QTa[0][0:8, o + 144:o + 272]) for o in (128, 400)]
    logit_f = af.alloc("logit_f", 8)
    sc1 = Osb[:, 0:256]
    sc2 = Osb[:, 256:512]

    Osb16 = Osb.bitcast(BF16)
    ew_slots = ([fw1t_r[:, i * 512:(i + 1) * 512] for i in range(4)]
                + [fw2t_r[:, i * 512:(i + 1) * 512] for i in range(4)]
                + [Abuf[0][:, 0:512], Abuf[0][:, 512:1024],
                   Abuf[1][:, 0:512], Abuf[1][:, 512:1024],
                   Vb[1][:, 0:512], Vb[1][:, 512:1024],
                   QTa[1][:, 0:512], QTa[1][:, 512:1024]])
    fw_slots = [Osb16[:, 0:1024], Osb16[:, 1024:2048], Ostg[:, 0:1024],
                fw1t_r[:, 0:1024], fw1t_r[:, 1024:2048],
                fw2t_r[:, 0:1024], fw2t_r[:, 1024:2048],
                Abuf[0][:, 0:1024], Abuf[1][:, 0:1024],
                Vb[1][:, 0:1024], Vb[1][:, 1024:2048],
                QTa[1][:, 0:1024],
                KTa[0][:, 0:1024], KTa[0][:, 1024:2048],
                KTa[1][:, 2048:3072], KTa[1][:, 3072:4096]]

    import functools
    top = functools.partial(b.op, "tensor")
    vop = functools.partial(b.op, "vector")
    sop = functools.partial(b.op, "scalar")
    K_RSQRT = 0x5f3759df

    def rsqrt_newton(dst, var_c, deps, st):
        tm1 = vop(lambda v: v.tensor_scalar(st[:, 6:7], var_c, 0.5, 0.5 * EPS,
                                            ALU.mult, ALU.add), deps=deps, inc=True)
        ti = vop(lambda v: v.tensor_scalar(st[:, 7:8].bitcast(mybir.dt.int32),
                                           var_c.bitcast(mybir.dt.int32), 1, None,
                                           ALU.arith_shift_right), deps=deps + [tm1], inc=True)
        ty = vop(lambda v: v.tensor_scalar(dst.bitcast(mybir.dt.int32),
                                           st[:, 7:8].bitcast(mybir.dt.int32),
                                           -1, K_RSQRT, ALU.mult, ALU.add),
                 deps=[ti], inc=True)
        t = ty
        for _ in range(3):
            t1 = vop(lambda v: v.tensor_tensor(st[:, 7:8], dst, dst, ALU.mult),
                     deps=[t], inc=True)
            t2 = vop(lambda v: v.tensor_tensor(st[:, 7:8], st[:, 7:8], st[:, 6:7],
                                               ALU.mult), deps=[t1], inc=True)
            t3 = vop(lambda v: v.tensor_scalar(st[:, 7:8], st[:, 7:8], -1.0, 1.5,
                                               ALU.mult, ALU.add), deps=[t2], inc=True)
            t = vop(lambda v: v.tensor_tensor(dst, dst, st[:, 7:8], ALU.mult),
                    deps=[t3], inc=True)
        return t

    def ln_stats(src_ap, mean_c, rs_c, deps, garbage, st=None):
        if st is None:
            st = stat
        tsum = vop(lambda v: v.reduce_sum(st[:, 0:1], src_ap, axis=AX.X),
                   deps=deps, inc=True)
        tssq = sop(lambda sc: sc.activation(out=garbage, in_=src_ap, func=AF.Square,
                                            accum_out=st[:, 1:2]),
                   deps=deps + [tsum], inc=True)
        tmean = vop(lambda v: v.tensor_scalar(mean_c, st[:, 0:1], 1.0 / D, None,
                                              ALU.mult), deps=[tsum], inc=True)
        tmsq = vop(lambda v: v.tensor_tensor(st[:, 2:3], mean_c, mean_c, ALU.mult),
                   deps=[tmean], inc=True)
        tvar = vop(lambda v: v.scalar_tensor_tensor(st[:, 3:4], st[:, 1:2], 1.0 / D,
                                                    st[:, 2:3], ALU.mult, ALU.subtract),
                   deps=[tmsq, tssq], inc=True)
        trs = rsqrt_newton(rs_c, st[:, 3:4], [tvar], st)
        return tmean, trs, tssq

    p3s = {
        "sc": None,
        "stores": [],
        "b45": [],
        "bA": None, "bB": None,
        "b6": None,
        "hsT": None,
        "eqfree": [None, None],
        "headA": [None, None],
        "osb": None, "ab0": None, "ab1": None, "vbq": None,
        "kta0": None, "kta1": None, "fwfree": None, "st": None,
    }

    def first_layer_T(dst_view, nchunk, w_view, bias_s, bias_p, rhs_views, kdeps,
                      psum_offs, nK=8, edeps=()):
        outs = []
        parts = dst_view.shape[0]
        for c in range(nchunk):
            ps = pf[0:parts, psum_offs[c]: psum_offs[c] + 128]
            mm = top(lambda t, c=c, ps=ps: mm_chain(
                t, ps, [w_view[:, kc, c * 128:(c + 1) * 128] for kc in range(nK)],
                [rhs_views[kc] for kc in range(nK)]), deps=kdeps, inc=True)
            th = sop(lambda sc_, ps=ps, c=c: sc_.activation(
                out=dst_view[:, c, :], in_=ps, func=AF.Gelu,
                bias=bias_p[0:parts, c:c + 1]), deps=[mm] + list(edeps), inc=True)
            outs.append((mm, th))
        return outs

    rw1bv = rw1b.rearrange("p (c n) -> p c n", c=8)
    hw1bv = hw1b.rearrange("p (c n) -> p c n", c=8)
    sw1bv = sw1b.rearrange("p (c n) -> p c n", c=16)
    rw2bv = rw2b[:, 0:16].rearrange("p (c n) -> p c n", c=2)
    hw2bv = hw2b[:, 0:2].rearrange("p (c n) -> p c n", c=2)
    projwTv = projwT.rearrange("p (c n) -> p c n", c=8)
    SCL = []
    for _p in range(2):
        _s = stat2[_p]
        SCL.append(dict(m1=_s[:, 8:9], m2=_s[:, 9:10], d=_s[:, 10:11],
                        w0=_s[:, 11:12], w1=_s[:, 12:13], hlt=_s[:, 13:14],
                        hD=_s[:, 14:15], mean=_s[:, 4:5], rs=_s[:, 5:6]))

    def ffn_head_a(qs, blk, ot_dep):
        tok0 = qs * QS + blk * BLK
        p = blk & 1
        S_ = dict(slots)
        S_.update(par[p])
        X = SCL[p]
        st = stat2[p]
        eq = EQ[p]
        hA = [p3s["headA"][p]]
        t_x = dma(S_["sB"], xb[tok0:tok0 + 128, :], deps=hA + [ot_dep])
        mmp = top(lambda t: (mm_chain(
            t, pf[:, 2048:2560],
            [OT[:, kc * 1024 + blk * 128:kc * 1024 + blk * 128 + 128] for kc in range(8)],
            [projwTv[:, kc, 0:512] for kc in range(8)], stop_last=False),
            mm_chain(t, pf[:, 2560:3072],
                     [OT[:, kc * 1024 + blk * 128:kc * 1024 + blk * 128 + 128]
                      for kc in range(8)],
                     [projwTv[:, kc, 512:1024] for kc in range(8)], stop_last=False),
            t.matmul(pf[:, 2048:2560], onesb[0:1, :], projb_r[0:1, 0:512], start=False,
                     stop=True),
            t.matmul(pf[:, 2560:3072], onesb[0:1, :], projb_r[0:1, 512:1024],
                     start=False, stop=True))[-1],
            deps=[ot_dep, t_projwT[-1], t_pbr, t_ms] + p3s["b45"], inc=True)
        tm2_, tr2_, tq2_ = ln_stats(pf[:, 2048:3072], X["mean"], X["rs"],
                                    [mmp, p3s["sc"], p3s["osb"]] + hA, Osb, st)
        tu2 = vop(lambda v: v.tensor_scalar(S_["sA"], pf[:, 2048:3072], X["mean"],
                                            X["rs"], ALU.subtract, ALU.mult),
                  deps=[tm2_, tr2_], inc=True)
        tx2 = vop(lambda v: v.tensor_tensor(S_["sA"], S_["sA"], S_["sB"], ALU.add),
                  deps=[tu2, t_x], inc=True)
        t_spx2 = dma(s_x2[tok0:tok0 + 128, :], S_["sA"], deps=[tx2])
        tm3_, tr3_, tq3_ = ln_stats(S_["sA"], X["mean"], X["rs"], [tx2, tq2_], Osb, st)
        txf = vop(lambda v: v.tensor_scalar(S_["xf"], S_["sA"], X["mean"], X["rs"],
                                            ALU.subtract, ALU.mult),
                  deps=[tm3_, tr3_], inc=True)
        txfb = vop(lambda v: v.tensor_copy(xfb, S_["xf"]), deps=[txf, p3s["ab0"]],
                   inc=True)
        return dict(t_x=t_x, mmp=mmp, tx2=tx2, t_spx2=t_spx2, txf=txf, txfb=txfb,
                    int_w=None, v_w=None, S_=S_, X=X, st=st, eq=eq, p=p,
                    sA_busy=[txf, t_spx2], sB_busy=[tx2], sC_busy=[])

    def ffn_head_a2(C):
        ttr = top(lambda t: [t.transpose(pb[:, c * 128:(c + 1) * 128],
                                         xfb[:, c * 128:(c + 1) * 128], identb)
                             for c in range(8)][-1],
                  deps=[C["txfb"], t_id, pb_last[0]], inc=True)
        txfT = vop(lambda v: v.tensor_copy(xfT, pb[:, 0:1024]),
                   deps=[ttr, p3s["ab1"]], inc=True)
        pb_last[0] = txfT
        C["txfT"] = txfT
        return C

    def ffn_head_b(qs, blk, C):
        S_, X, eq, p = C["S_"], C["X"], C["eq"], C["p"]
        txfT = C["txfT"]
        C["pre0"] = [dma(ew_slots[k], s_ew1[k], deps=[t_sew1[k], p3s["fwfree"]])
                     for k in range(8)]
        r_outs = first_layer_T(h1Tv, 2, rw1bv, rb1s_pp, rb1_pp,
                               [xfTv[:, kc, :] for kc in range(8)],
                               [txfT, t_rw1], [2048, 2560],
                               edeps=[p3s["kta1"]])
        mml = top(lambda t: (mm_chain(
            t, pf[:, 2560:2568], [h1Tv[:, c2, :] for c2 in range(2)],
            [rw2bv[:, c2, :] for c2 in range(2)], stop_last=False),
            t.matmul(pf[:, 2560:2568], onesb[0:1, :], rb2_r[0:1, :], start=False,
                     stop=True))[-1],
            deps=[r_outs[-1][1], t_rw2, t_rb2], inc=True)
        tlg = vop(lambda v: v.tensor_copy(logit_f, pf[:, 2560:2568]), deps=[mml], inc=True)
        tmx1 = vop(lambda v: v.reduce_max(X["m1"], logit_f, axis=AX.X), deps=[tlg],
                   inc=True)
        teq1 = vop(lambda v: v.tensor_tensor(sc1[:, 0:8], logit_f,
                                             X["m1"].broadcast_to([128, 8]),
                                             ALU.is_equal),
                   deps=[tmx1, p3s["sc"]], inc=True)
        tmsk = vop(lambda v: v.scalar_tensor_tensor(sc1[:, 8:16], sc1[:, 0:8], -1e30,
                                                    logit_f, ALU.mult, ALU.add),
                   deps=[teq1], inc=True)
        tmx2 = vop(lambda v: v.reduce_max(X["m2"], sc1[:, 8:16], axis=AX.X), deps=[tmsk],
                   inc=True)
        teq2 = vop(lambda v: v.tensor_tensor(sc1[:, 16:24], sc1[:, 8:16],
                                             X["m2"].broadcast_to([128, 8]),
                                             ALU.is_equal),
                   deps=[tmx2], inc=True)
        tdd = vop(lambda v: v.tensor_tensor(X["d"], X["m1"], X["m2"], ALU.subtract),
                  deps=[tmx2], inc=True)
        tw0 = sop(lambda sc_: sc_.activation(out=X["w0"], in_=X["d"], func=AF.Sigmoid),
                  deps=[tdd], inc=True)
        tw1 = vop(lambda v: v.tensor_scalar(X["w1"], X["w0"], -1.0, 1.0, ALU.mult,
                                            ALU.add), deps=[tw0], inc=True)
        teqb = vop(lambda v: (v.tensor_copy(eq["b"][:, 0:8], sc1[:, 0:8]),
                              v.tensor_copy(eq["b"][:, 8:16], sc1[:, 16:24]))[-1],
                   deps=[teq1, teq2, p3s["eqfree"][p]], inc=True)
        p3s["sc"] = teqb
        C.update(mml=mml, tw0=tw0, tw1=tw1, teqb=teqb, tlg=tlg,
                 r_b4=[r[1] for r in r_outs],
                 b5_free=tlg, b5m_free=tlg)
        return C

    def ctrl_mm_fn(group_off, pg, src, eq):
        def fn(t):
            last = None
            for nt in range(2):
                ps = pf[:, group_off + nt * 512:group_off + (nt + 1) * 512]
                col = pg * 1024 + nt * 512
                if src == "s":
                    t.matmul(ps, hsT, sw2b[0:64, col:col + 512], start=True, stop=False)
                    last = t.matmul(ps, onesb[0:1, :], sb2_r[0:1, col:col + 512],
                                    start=False, stop=True)
                else:
                    hmv = hm0Tv if src == "0" else hm1Tv
                    eqt = eq["T0"] if src == "0" else eq["T1"]
                    for hc in range(4):
                        t.matmul(ps, hmv[:, hc, :], ew2Rv[:, hc, col:col + 512],
                                 start=(hc == 0), stop=False)
                    last = t.matmul(ps, eqt, eb2b[0:8, col:col + 512],
                                    start=False, stop=True)
            return last
        return fn

    def ffn_iter(qs, blk, it, C):
        S_ = C["S_"]
        X = C["X"]
        eq = C["eq"]
        if it == 0:
            C["bA_free"] = p3s["bA"]
            C["bB_free"] = p3s["bB"]
            C["b6_free"] = p3s["b6"]
        if it == 0:
            intTu, vTu = xfTv, None
            t_int_T = C["txfT"]
            nKc = 8
        else:
            tib = vop(lambda v: v.tensor_copy(intb, S_["intg"]), deps=[C["int_w"]],
                      inc=True)
            ttri = top(lambda t: [t.transpose(pb[:, c * 128:(c + 1) * 128],
                                              intb[:, c * 128:(c + 1) * 128], identb)
                                  for c in range(8)][-1],
                       deps=[tib, pb_last[0]], inc=True)
            tevi = vop(lambda v: v.tensor_copy(intTb, pb[:, 0:1024]), deps=[ttri],
                       inc=True)
            tvb = vop(lambda v: v.tensor_copy(vbb, S_["vf"]),
                      deps=[C["v_w"], p3s["kta0"]], inc=True)
            ttrv = top(lambda t: [t.transpose(pb[:, c * 128:(c + 1) * 128],
                                              vbb[:, c * 128:(c + 1) * 128], identb)
                                  for c in range(8)][-1],
                       deps=[tvb, tevi], inc=True)
            tevv = vop(lambda v: v.tensor_copy(vTb, pb[:, 0:1024]), deps=[ttrv], inc=True)
            pb_last[0] = tevv
            intTu, vTu = intTv, vTv
            t_int_T = tevv
            nKc = 16

        def rhs_kc(kc):
            return intTu[:, kc, :] if kc < 8 else vTu[:, kc - 8, :]

        ttf = vop(lambda v: v.tensor_tensor(
            S_["tf"], S_["xf"] if it == 0 else S_["intg"], mubc, ALU.subtract),
            deps=[C["txf"] if it == 0 else C["int_w"], t_mubc], inc=True)
        h_outs = first_layer_T(hhTv, 2, hw1bv, hb1s_pp, hb1_pp,
                               [rhs_kc(kc) for kc in range(8)],
                               [t_int_T, t_hw1, C["b5_free"]] + list(C["r_b4"]),
                               [2048, 2560], nK=8)
        mmh = top(lambda t: (mm_chain(
            t, pf[:, 2568:2569], [hhTv[:, c2, :] for c2 in range(2)],
            [hw2bv[:, c2, :] for c2 in range(2)], stop_last=False),
            t.matmul(pf[:, 2568:2569], onesb[0:1, :], hb2_r[0:1, 0:1], start=False,
                     stop=True))[-1],
            deps=[h_outs[-1][1], t_hw2, t_hb2, C["b5_free"]], inc=True)
        thl = sop(lambda sc_: sc_.activation(out=X["hlt"], in_=pf[:, 2568:2569],
                                             func=AF.Sigmoid), deps=[mmh], inc=True)
        thd = vop(lambda v: v.tensor_scalar(X["hD"], X["hlt"], 0.1, None, ALU.mult),
                  deps=[thl], inc=True)
        mmsh = top(lambda t: mm_chain(
            t, pf[0:64, 3072:3200], [sw1bv[:, kc, :] for kc in range(nKc)],
            [rhs_kc(kc) for kc in range(nKc)]),
            deps=[t_int_T, t_sw1, C["b6_free"]], inc=True)
        thsh = sop(lambda sc_: sc_.activation(
            out=hsT, in_=pf[0:64, 3072:3200], func=AF.Gelu,
            bias=sb1_pp[0:64, 0:1]), deps=[mmsh, p3s["hsT"]], inc=True)
        klist = list(range(16)) if it == 1 else list(range(8))
        h_all_toks = [h[1] for h in h_outs]
        mm_last = None
        for ki, kc in enumerate(klist):
            wt = C["pre0"][kc] if kc < 8 else C["pre1"][kc - 8]
            sl = ew_slots[kc]
            mm_last = top(lambda t, kc=kc, ki=ki, sl=sl: [t.matmul(
                pf[:, hc * 512:hc * 512 + 128],
                sl[:, hc * 128:(hc + 1) * 128], rhs_kc(kc),
                start=(ki == 0), stop=(ki == len(klist) - 1)) for hc in range(4)][-1],
                deps=[wt, t_int_T] + (h_all_toks + [C["bA_free"], C["bB_free"]]
                                      if ki == 0 else []), inc=True)
        if it == 0:
            C["pre1"] = [dma(ew_slots[8 + i], s_ew1[8 + i],
                             deps=[t_sew1[8 + i], mm_last]
                             + ([C["txfT"]] if i < 2 else [])
                             + ([p3s["vbq"]] if i >= 4 else []))
                         for i in range(8)]
        else:
            C["preF1"] = [dma(fw_slots[j], s_fw1[j],
                              deps=[t_sfw1[j], mm_last]
                              + ([C["teqb"]] if j < 3 else []))
                          for j in range(14)]
        if it == 0:
            ttre = top(lambda t: (t.transpose(pb[0:8, 0:128], eq["b"][:, 0:8], identb),
                                  t.transpose(pb[0:8, 128:256], eq["b"][:, 8:16],
                                              identb))[-1],
                       deps=[C["teqb"], pb_last[0]], inc=True)
            teqT = vop(lambda v: (v.tensor_copy(eq["T0"], pb[0:8, 0:128]),
                                  v.tensor_copy(eq["T1"], pb[0:8, 128:256]))[-1],
                       deps=[ttre], inc=True)
            pb_last[0] = teqT
            C["teqT"] = teqT
        mask_deps = [C["teqT"]]
        hm_toks = []
        ex_toks = []
        for hc in range(4):
            ps = pf[:, hc * 512:hc * 512 + 128]
            th = sop(lambda sc_, ps=ps, hc=hc: sc_.activation(
                out=hallTv[:, hc, :], in_=ps, func=AF.Gelu,
                bias=eb1_pp[:, hc:hc + 1]), deps=[mm_last], inc=True)
            mmm0 = top(lambda t, hc=hc: t.matmul(
                pf[:, 2560:2688], sselb[:, hc * 128:(hc + 1) * 128], eq["T0"],
                start=True, stop=True),
                deps=[t_sselb, C["teqT"], C["b5m_free"], thl] + h_all_toks, inc=True)
            thm0 = vop(lambda v, hc=hc: v.tensor_tensor(
                hm0Tv[:, hc, :], hallTv[:, hc, :], pf[:, 2560:2688], ALU.mult),
                deps=[mmm0, th], inc=True)
            mmm1 = top(lambda t, hc=hc: t.matmul(
                pf[:, 2688:2816], sselb[:, hc * 128:(hc + 1) * 128], eq["T1"],
                start=True, stop=True),
                deps=[t_sselb, C["teqT"], thm0, thl], inc=True)
            thm1 = vop(lambda v, hc=hc: v.tensor_tensor(
                hm1Tv[:, hc, :], hallTv[:, hc, :], pf[:, 2688:2816], ALU.mult),
                deps=[mmm1, th], inc=True)
            C["b5m_free"] = thm1
            hm_toks += [thm0, thm1]
            ex_toks += [th]
        A, B = 0, 1024
        bA = [C["bA_free"]]
        bB = [C["bB_free"]]
        bC = [C.get("bC_free")]
        CC = 2048

        def round_mm(pg):
            mms = top(ctrl_mm_fn(A, pg, "s", eq),
                      deps=[thsh, t_sb2, bA[0]] + ex_toks, inc=True)
            mm0 = top(ctrl_mm_fn(B, pg, "0", eq),
                      deps=hm_toks + [t_eb2, bB[0]] + ex_toks, inc=True)
            return mms, mm0

        if it == 1:
            mms, mm0 = round_mm(0)
            evs = sop(lambda sc_: sc_.activation(
                out=S_["sA"], in_=pf[:, A:A + 1024], func=AF.Sigmoid),
                deps=[mms] + C["sA_busy"], inc=True)
            tvsh = vop(lambda v: v.tensor_tensor(S_["vsh"], S_["sA"], S_["vf"], ALU.mult),
                       deps=[evs, C["v_w"]], inc=True)
            mm1 = top(ctrl_mm_fn(CC, 0, "1", eq), deps=hm_toks + [t_eb2, bC[0]],
                      inc=True)
            ev0 = sop(lambda sc_: sc_.activation(
                out=S_["sB"], in_=pf[:, B:B + 1024], func=AF.Sigmoid),
                deps=[mm0] + C["sB_busy"], inc=True)
            tvn0 = vop(lambda v: v.tensor_tensor(S_["vn0"], S_["sB"], S_["vf"], ALU.mult),
                       deps=[ev0], inc=True)
            ev1 = sop(lambda sc_: sc_.activation(
                out=S_["sC"], in_=pf[:, CC:CC + 1024], func=AF.Sigmoid),
                deps=[mm1] + C["sC_busy"], inc=True)
            bC[0] = ev1
            tvn1 = vop(lambda v: v.tensor_tensor(S_["vn1"], S_["sC"], S_["vf"], ALU.mult),
                       deps=[ev1], inc=True)
            a_done = [tvsh, tvn0, tvn1]
        else:
            a_done = []
        if it == 1:
            bA[0], bB[0] = ev1, ev0
            sA_war, sB_war, sC_war = [tvsh], [tvn0], [tvn1]
        else:
            sA_war, sB_war, sC_war = C["sA_busy"], C["sB_busy"], C["sC_busy"]
        mms, mm0 = round_mm(1)
        evs = sop(lambda sc_: sc_.activation(
            out=S_["sA"], in_=pf[:, A:A + 1024], func=AF.Sigmoid, scale=-1.0),
            deps=[mms] + sA_war, inc=True)
        mm1 = top(ctrl_mm_fn(CC, 1, "1", eq), deps=hm_toks + [t_eb2, bC[0]], inc=True)
        ev0 = sop(lambda sc_: sc_.activation(
            out=S_["sB"], in_=pf[:, B:B + 1024], func=AF.Sigmoid, scale=-1.0),
            deps=[mm0] + sB_war, inc=True)
        ev1 = sop(lambda sc_: sc_.activation(
            out=S_["sC"], in_=pf[:, CC:CC + 1024], func=AF.Sigmoid, scale=-1.0),
            deps=[mm1] + sC_war, inc=True)
        bC[0] = ev1
        tlnS = sop(lambda sc_: sc_.activation(out=S_["sA"], in_=S_["sA"], func=AF.Ln),
                   deps=[evs] + a_done, inc=True)
        tln0 = sop(lambda sc_: sc_.activation(out=S_["sB"], in_=S_["sB"], func=AF.Ln),
                   deps=[ev0], inc=True)
        tln1 = sop(lambda sc_: sc_.activation(out=S_["sC"], in_=S_["sC"], func=AF.Ln),
                   deps=[ev1], inc=True)
        if it == 0:
            tvsh = vop(lambda v: v.tensor_tensor(S_["vsh"], S_["sA"], S_["tf"],
                                                 ALU.mult), deps=[tlnS, ttf], inc=True)
            tvn0 = vop(lambda v: v.tensor_tensor(S_["vn0"], S_["sB"], S_["tf"],
                                                 ALU.mult),
                       deps=[tln0, ttf, p3s["st"]], inc=True)
            tvn1 = vop(lambda v: v.tensor_tensor(S_["vn1"], S_["sC"], S_["tf"],
                                                 ALU.mult), deps=[tln1, ttf], inc=True)
        else:
            tu = vop(lambda v: v.tensor_tensor(S_["sA"], S_["sA"], S_["tf"], ALU.mult),
                     deps=[tlnS, ttf, tvsh], inc=True)
            tvsh = vop(lambda v: v.tensor_tensor(S_["vsh"], S_["vsh"], S_["sA"], ALU.add),
                       deps=[tu, tvsh], inc=True)
            tu0 = vop(lambda v: v.tensor_tensor(S_["sB"], S_["sB"], S_["tf"], ALU.mult),
                      deps=[tln0, ttf, tvn0], inc=True)
            tvn0 = vop(lambda v: v.tensor_tensor(S_["vn0"], S_["vn0"], S_["sB"], ALU.add),
                       deps=[tu0, tvn0], inc=True)
            tu1 = vop(lambda v: v.tensor_tensor(S_["sC"], S_["sC"], S_["tf"], ALU.mult),
                      deps=[tln1, ttf, tvn1], inc=True)
            tvn1 = vop(lambda v: v.tensor_tensor(S_["vn1"], S_["vn1"], S_["sC"], ALU.add),
                       deps=[tu1, tvn1], inc=True)
        bA[0], bB[0] = ev1, ev0
        mms, mm0 = round_mm(2)
        evs = sop(lambda sc_: sc_.activation(
            out=S_["sA"], in_=pf[:, A:A + 1024], func=AF.Sigmoid),
            deps=[mms, tvsh, tvn0 if it == 0 else tvsh], inc=True)
        tzs = vop(lambda v: v.tensor_tensor(S_["sB"], S_["sA"], S_["vsh"], ALU.mult),
                  deps=[evs, tvsh, tvn0], inc=True)
        ev0g = sop(lambda sc_: sc_.activation(
            out=S_["sA"], in_=pf[:, B:B + 1024], func=AF.Sigmoid),
            deps=[mm0, tzs], inc=True)
        tacc = vop(lambda v: v.scalar_tensor_tensor(
            S_["acc1"], S_["sA"], X["w0"], S_["vn0"], ALU.mult, ALU.mult),
            deps=[ev0g, tvn0, C["tw0"]], inc=True)
        mm1 = top(ctrl_mm_fn(CC, 2, "1", eq), deps=hm_toks + [t_eb2, bC[0]], inc=True)
        ev1g = sop(lambda sc_: sc_.activation(
            out=S_["sC"], in_=pf[:, CC:CC + 1024], func=AF.Sigmoid),
            deps=[mm1, tvn1], inc=True)
        bC[0] = ev1g
        tu1g = vop(lambda v: v.scalar_tensor_tensor(
            S_["sC"], S_["sC"], X["w1"], S_["vn1"], ALU.mult, ALU.mult),
            deps=[ev1g, tvn1, C["tw1"]], inc=True)
        tacc = vop(lambda v: v.tensor_tensor(S_["acc1"], S_["acc1"], S_["sC"], ALU.add),
                   deps=[tacc, tu1g], inc=True)
        ev1 = ev1g
        ev0 = ev0g
        tq1 = vop(lambda v: v.tensor_scalar(S_["sB"], S_["sB"], sw_ap, None, ALU.mult),
                  deps=[tzs, t_swc], inc=True)
        tq2 = vop(lambda v: v.scalar_tensor_tensor(
            S_["acc1"], S_["acc1"], osw_ap, S_["sB"], ALU.mult, ALU.add),
            deps=[tacc, tq1, t_osw], inc=True)
        tint = vop(lambda v: v.scalar_tensor_tensor(
            S_["intg"], S_["acc1"], X["hD"], S_["xf"] if it == 0 else S_["intg"],
            ALU.mult, ALU.add),
            deps=[tq2, thd] + ([C["txf"]] if it == 0 else [C["int_w"]]), inc=True)
        C["int_w"] = tint
        if it == 0:
            tv1 = vop(lambda v: v.tensor_scalar(S_["vn0"], S_["vn0"], X["w0"], None,
                                                ALU.mult), deps=[tvn0, C["tw0"], tacc],
                      inc=True)
            tv2 = vop(lambda v: v.scalar_tensor_tensor(
                S_["vn0"], S_["vn1"], X["w1"], S_["vn0"], ALU.mult, ALU.add),
                deps=[tv1, tvn1, C["tw1"]], inc=True)
            tv3 = vop(lambda v: v.tensor_scalar(S_["vsh"], S_["vsh"], sw_ap, None,
                                                ALU.mult), deps=[tzs, tvsh, t_swc],
                      inc=True)
            tv4 = vop(lambda v: v.scalar_tensor_tensor(
                S_["vsh"], S_["vn0"], osw_ap, S_["vsh"], ALU.mult, ALU.add),
                deps=[tv2, tv3, t_osw], inc=True)
            tv5 = vop(lambda v: v.tensor_scalar(S_["vf"], S_["vsh"], X["hlt"], None,
                                                ALU.mult), deps=[tv4, thl], inc=True)
            C["v_w"] = tv5
        C["bA_free"] = ev1
        C["bB_free"] = ev0
        C["b5_free"] = C["b5m_free"]
        C["b6_free"] = thsh
        C["r_b4"] = h_all_toks + hm_toks[-2:]
        C["sA_busy"] = [tacc]
        C["sB_busy"] = [tq2]
        C["sC_busy"] = [tacc]
        C["ttf"] = ttf
        if it == 1:
            for j in (14, 15):
                C["preF1"].append(dma(fw_slots[j], s_fw1[j],
                                      deps=[t_sfw1[j], mm1, mm0, mmh]))
            p3s["b45"] = [ev1g, hm_toks[-1]]
            p3s["hsT"] = mms
            p3s["eqfree"][C["p"]] = mm1
        C["bC_free"] = bC[0]
        return C

    FW2S = [2, 3, 4, 5, 6, 9, 10, 11, 12, 13, 14, 15, 2, 3, 4, 5]
    _FW2INV = {s: k for k, s in enumerate(FW2S[:12])}

    def ffn_refine_a(qs, blk, C):
        S_ = C["S_"]
        tib = vop(lambda v: v.tensor_copy(intb, S_["intg"]), deps=[C["int_w"]], inc=True)
        ttri = top(lambda t: [t.transpose(pb[:, c * 128:(c + 1) * 128],
                                          intb[:, c * 128:(c + 1) * 128], identb)
                              for c in range(8)][-1],
                   deps=[tib, pb_last[0]], inc=True)
        tevi = vop(lambda v: v.tensor_copy(intTb, pb[:, 0:1024]), deps=[ttri], inc=True)
        pb_last[0] = tevi
        mmr = None
        mmr_l = []
        pf2_l = [None] * 16
        for ti in range(16):
            kc, half = ti // 2, ti % 2
            mmr = top(lambda t, kc=kc, half=half, ti=ti: [t.matmul(
                pf[:, half * 1024 + nt * 512:half * 1024 + (nt + 1) * 512],
                intTv[:, kc, :], fw_slots[ti][:, nt * 512:(nt + 1) * 512],
                start=(kc == 0), stop=False) for nt in range(2)][-1],
                deps=[C["preF1"][ti], tevi]
                + ([C["bA_free"], C["bB_free"]] if ti == 0 else []),
                inc=True)
            mmr_l.append(mmr)
            if ti in _FW2INV:
                k2 = _FW2INV[ti]
                pf2_l[k2] = dma(fw_slots[ti], s_fw2[k2], deps=[t_sfw2[k2], mmr])
        mmrb = top(lambda t: [t.matmul(
            pf[:, nt * 512:(nt + 1) * 512], onesb64[0:1, :],
            fb1_r[0:1, nt * 512:(nt + 1) * 512], start=False, stop=True)
            for nt in range(4)][-1], deps=[mmr, t_fb1r, t_ms64], inc=True)
        hr_toks = []
        for hf in range(2):
            thr = sop(lambda sc_, hf=hf: sc_.activation(
                out=hrb[:, hf * 1024:(hf + 1) * 1024],
                in_=pf[:, hf * 1024:(hf + 1) * 1024], func=AF.Gelu),
                deps=[mmrb], inc=True)
            hr_toks.append(thr)
        tr_toks = []
        for r in range(2):
            ttr2 = top(lambda t, r=r: [t.transpose(
                pb[:, c * 128:(c + 1) * 128],
                hrb[:, (r * 8 + c) * 128:(r * 8 + c + 1) * 128], identb)
                for c in range(8)][-1],
                deps=[hr_toks[r], pb_last[0]], inc=True)
            tev2 = vop(lambda v, r=r: v.tensor_copy(
                hrT[:, r * 1024:(r + 1) * 1024], pb[:, 0:1024]), deps=[ttr2], inc=True)
            pb_last[0] = tev2
            tr_toks.append(tev2)
        p3s["osb"] = mmr_l[1]
        p3s["ab0"] = mmr_l[7]
        p3s["ab1"] = mmr_l[8]
        return dict(mmr_l=mmr_l, pf2=pf2_l, hr=hr_toks, tr=tr_toks)

    def ffn_refine_b1(qs, blk, C, R):
        mm2_l = []
        for kc in range(16):
            sl = FW2S[kc]
            mm2 = top(lambda t, kc=kc, sl=sl: [t.matmul(
                pf[:, nt * 512:(nt + 1) * 512], hrTv[:, kc, :],
                fw_slots[sl][:, nt * 512:(nt + 1) * 512],
                start=(kc == 0), stop=False) for nt in range(2)][-1],
                deps=[R["pf2"][kc], R["tr"][kc // 8]]
                + ([R["hr"][-1]] if kc == 0 else []),
                inc=True)
            mm2_l.append(mm2)
            if kc < 4:
                k2 = 12 + kc
                R["pf2"][k2] = dma(fw_slots[FW2S[k2]], s_fw2[k2],
                                   deps=[t_sfw2[k2], mm2])
        R["mm2_l"] = mm2_l

    def ffn_refine_b2(qs, blk, C, R):
        S_ = C["S_"]
        X = C["X"]
        tok0 = qs * QS + blk * BLK
        mm2_l = R["mm2_l"]
        hr_toks = R["hr"]
        mm2b = top(lambda t: [t.matmul(
            pf[:, nt * 512:(nt + 1) * 512], onesb[0:1, :],
            fb2_r[0:1, nt * 512:(nt + 1) * 512], start=False, stop=True)
            for nt in range(2)][-1], deps=[mm2_l[-1], t_fb2], inc=True)
        tiw = vop(lambda v: v.tensor_tensor(S_["sA"], pf[:, 0:1024], iwbc, ALU.mult),
                  deps=[mm2b, t_iwbc] + C["sA_busy"], inc=True)
        tffo = vop(lambda v: v.tensor_tensor(S_["sA"], S_["sA"], S_["xf"], ALU.add),
                   deps=[tiw, C["txf"]], inc=True)
        tm4_, tr4_, tq4_ = ln_stats(S_["sA"], X["mean"], X["rs"], [tffo], S_["sC"],
                                    C["st"])
        t_lx2 = dma(S_["sB"], s_x2[tok0:tok0 + 128, :],
                    deps=[C["t_spx2"]] + C["sB_busy"])
        tu4 = vop(lambda v: v.tensor_scalar(S_["sA"], S_["sA"], X["mean"], X["rs"],
                                            ALU.subtract, ALU.mult),
                  deps=[tm4_, tr4_], inc=True)
        tout = vop(lambda v: v.tensor_tensor(S_["vn0"], S_["sA"], S_["sB"], ALU.add),
                   deps=[tu4, t_lx2], inc=True)
        st = dma(out[tok0:tok0 + 128, :], S_["vn0"], deps=[tout])
        p3s["stores"].append(st)
        p3s["bA"] = tiw
        p3s["bB"] = hr_toks[1]
        p3s["fwfree"] = mm2_l[15]
        p3s["vbq"] = mm2_l[7]
        p3s["kta0"] = mm2_l[9]
        p3s["kta1"] = mm2_l[11]
        p3s["b6"] = C["b6_free"]
        p3s["st"] = st
        p3s["headA"][C["p"]] = st
        return st

    p3_tail = []
    nqs = 0 if STAGE == 0 else (1 if STAGE < 3 else 2)
    nblk_lim = NBLK
    if STAGE == 4:
        nqs = 1
    for qs in range(nqs):
        ot_toks = attention_qs(qs, p3_tail)
        if qs == 1 and DEBUG:
            dma(s_dbg, OT, deps=ot_toks[-2:])
        if STAGE < 2:
            break
        p3s["b45"] = [ep_free[0]]
        p3s["bA"] = sb_free[0]
        p3s["bB"] = sb_free[1]
        p3s["b6"] = ep6_free[0]
        Cn = None
        for blk in range(nblk_lim):
            if blk == 0:
                C = ffn_head_b(qs, 0, ffn_head_a2(ffn_head_a(qs, 0, ot_toks[-1])))
            else:
                C = Cn
            C = ffn_iter(qs, blk, 0, C)
            C = ffn_iter(qs, blk, 1, C)
            R = ffn_refine_a(qs, blk, C)
            if blk + 1 < nblk_lim:
                Cn = ffn_head_a(qs, blk + 1, ot_toks[-1])
            ffn_refine_b1(qs, blk, C, R)
            if blk + 1 < nblk_lim:
                Cn = ffn_head_a2(Cn)
            ffn_refine_b2(qs, blk, C, R)
            if blk + 1 < nblk_lim:
                Cn = ffn_head_b(qs, blk + 1, Cn)
        p3_tail = b.join()

    b.op("sync", lambda e: e.nop(),
         deps=list(p3s["stores"]) + b.join() + b.dma_fence("sync"))


    def mk_closure(ename):
        def closure(eng):
            waited = {}
            for drain, waits, fn, chan, amt in b.recs[ename]:
                if drain:
                    eng.drain()
                for ch, val in waits:
                    if waited.get(ch, 0) < val:
                        eng.wait_ge(sems[ch], val)
                        waited[ch] = val
                inst = fn(eng)
                if chan is not None:
                    inst.then_inc(sems[chan], amt)
        return closure

    for ename in ENGINES:
        getattr(block, ename)(mk_closure(ename))
    es.close()
    return nc


_NC_CACHE = []


def kernel(**inputs):
    if not _NC_CACHE:
        _NC_CACHE.append(build_nc())
    nc = _NC_CACHE[0]
    x = np.ascontiguousarray(np.asarray(inputs["L0_attn_x"], dtype=np.float32))
    f32 = lambda k: np.ascontiguousarray(np.asarray(inputs[k], dtype=np.float32))
    wmap = {
        "qkv_w": f32("L0_attn_qkv_w"), "qkv_b": f32("L0_attn_qkv_b"),
        "proj_w": f32("L0_attn_proj_w"), "proj_b": f32("L0_attn_proj_b"),
        "iw": f32("L0_ffn_integration_weight"), "mu": f32("L0_ffn_mu"),
        "rw1": f32("L0_ffn_router_w1"), "rb1": f32("L0_ffn_router_b1"),
        "rw2": f32("L0_ffn_router_w2"), "rb2": f32("L0_ffn_router_b2"),
        "hw1": f32("L0_ffn_halt_w1"), "hb1": f32("L0_ffn_halt_b1"),
        "hw2": f32("L0_ffn_halt_w2"), "hb2": f32("L0_ffn_halt_b2"),
        "ew1": f32("L0_ffn_expert_w1"), "eb1": f32("L0_ffn_expert_b1"),
        "ew2": f32("L0_ffn_expert_w2"), "eb2": f32("L0_ffn_expert_b2"),
        "sw1": f32("L0_ffn_shared_w1"), "sb1": f32("L0_ffn_shared_b1"),
        "sw2": f32("L0_ffn_shared_w2"), "sb2": f32("L0_ffn_shared_b2"),
        "swt": f32("L0_ffn_shared_weight").reshape(1, 1),
        "fw1": f32("L0_ffn_refine_w1"), "fb1": f32("L0_ffn_refine_b1"),
        "fw2": f32("L0_ffn_refine_w2"), "fb2": f32("L0_ffn_refine_b2"),
    }
    in_maps = []
    for c in range(8):
        bq, half = c // 2, c % 2
        if half == 0:
            xb_c = x[bq]
        else:
            xb_c = np.concatenate([x[bq, 2048:], x[bq, :2048]], axis=0)
        in_maps.append({"xb": np.ascontiguousarray(xb_c), **wmap})
    res = run_bass_kernel_spmd(nc, in_maps, core_ids=list(range(8)))
    outv = np.empty((4, 4096, 1024), np.float32)
    for c in range(8):
        bq, half = c // 2, c % 2
        outv[bq, half * 2048:(half + 1) * 2048] = np.asarray(res.results[c]["out"])
    return outv
